# version 09
# baseline (speedup 1.0000x reference)
import numpy as np
import ml_dtypes
from contextlib import ExitStack

import concourse.bass as bass
import concourse.mybir as mybir
from concourse.bass_utils import run_bass_kernel_spmd

F32 = mybir.dt.float32
BF16 = mybir.dt.bfloat16
AF = mybir.ActivationFunctionType
ALU = mybir.AluOpType
AX = mybir.AxisListType

D = 2048
T = 2048
NB = 8
NH = 16
NKV = 4
DKH = 128
NC_D = D // 128
NT_T = T // 128
INTER = 1408
NI = INTER // 128
NE = 16
TOPK = 4
EPS = 1e-5
BIG = 1.0e30
GSZ = 512
NG = T // GSZ
CHUNK = 1024
NCH = T // CHUNK
INVD = 1.0 / D
CONSTVALS = (0.0, 1.0, EPS, -0.5, -1.0, INVD, BIG, -BIG)

ENGINES = ("tensor", "vector", "scalar", "gpsimd", "sync")
DMA_SEM = {"tensor": "tdma", "vector": "vdma", "scalar": "adma",
           "gpsimd": "gdma", "sync": "sdma"}


class Bld:

    DMA_POOL = {"gdma": 16, "sdma": 16, "adma": 4, "vdma": 4, "tdma": 4}

    def __init__(self, nc, ctx):
        self.nc = nc
        self.ops = []
        self.blk = 0
        self.res = {}
        self.sem = {}
        for e in ENGINES:
            self.sem[e] = ctx.enter_context(nc.semaphore("s_" + e))
        for q, n in self.DMA_POOL.items():
            for i in range(n):
                self.sem[f"{q}{i}"] = ctx.enter_context(nc.semaphore(f"s_{q}{i}"))
        self.sem_count = {k: 0 for k in self.sem}
        self.dma_rr = {q: 0 for q in self.DMA_POOL}
        self.first_unemitted = 0

    def op(self, eng, fn, reads=(), writes=(), dma=False, hazard=False):
        idx = len(self.ops)
        deps = set()
        for r in reads:
            st = self.res.get(r)
            if st and st["w"] is not None:
                deps.add(st["w"])
        for w in writes:
            st = self.res.get(w)
            if st:
                if st["w"] is not None:
                    deps.add(st["w"])
                deps.update(st["r"])
        self.ops.append({"i": idx, "eng": eng, "fn": fn, "deps": deps,
                         "dma": dma, "blk": self.blk, "inc": False,
                         "hazard": hazard})
        for r in reads:
            self.res.setdefault(r, {"w": None, "r": []})["r"].append(idx)
        for w in writes:
            self.res[w] = {"w": idx, "r": []}
        return idx

    def mm(self, out, lhsT, rhs, start, stop, reads, writes):
        self.op("tensor", lambda t: t.matmul(out, lhsT, rhs, start=start, stop=stop),
                reads=reads, writes=writes)

    def transpose(self, out, in_, ident, reads, writes):
        self.op("tensor", lambda t: t.transpose(out, in_, ident),
                reads=reads, writes=writes)

    def tt(self, eng, out, a, c, alu, reads, writes, hazard=False):
        self.op(eng, lambda e: e.tensor_tensor(out, a, c, alu),
                reads=reads, writes=writes, hazard=hazard)

    def ts(self, eng, out, a, s1, s2, op0, op1, reads, writes, hazard=False):
        if op1 is None:
            self.op(eng, lambda e: e.tensor_scalar(out, a, s1, None, op0),
                    reads=reads, writes=writes, hazard=hazard)
        else:
            self.op(eng, lambda e: e.tensor_scalar(out, a, s1, s2, op0, op1),
                    reads=reads, writes=writes, hazard=hazard)

    def stt(self, eng, out, in0, scalar, in1, op0, op1, reads, writes, hazard=False):
        self.op(eng, lambda e: e.scalar_tensor_tensor(out, in0, scalar, in1, op0, op1),
                reads=reads, writes=writes, hazard=hazard)

    def copy(self, eng, out, in_, reads, writes, hazard=False):
        self.op(eng, lambda e: e.tensor_copy(out, in_), reads=reads, writes=writes,
                hazard=hazard)

    def act(self, out, in_, func, reads, writes, bias=0.0, scale=1.0, accum_out=None):
        self.op("scalar",
                lambda e: e.activation(out, in_, func, bias=bias, scale=scale,
                                       accum_out=accum_out),
                reads=tuple(reads) + ("constv",), writes=writes)

    def red(self, out, in_, op, reads, writes, negate=False):
        self.op("vector",
                lambda e: e.tensor_reduce(out, in_, AX.X, op, negate=negate),
                reads=reads, writes=writes, hazard=True)

    def recip(self, out, in_, reads, writes):
        self.op("vector", lambda e: e.reciprocal(out, in_), reads=reads, writes=writes,
                hazard=True)

    def dma(self, eng, out, in_, reads, writes):
        self.op(eng, lambda e: e.dma_start(out, in_), reads=reads, writes=writes,
                dma=True)

    def flush(self, name):
        seg = self.ops[self.first_unemitted:]
        if not seg:
            self.blk += 1
            return
        for o in seg:
            for d in o["deps"]:
                p = self.ops[d]
                if p["dma"]:
                    p["inc"] = True
                elif p["blk"] == o["blk"] and (p["eng"] != o["eng"] or p["hazard"]):
                    p["inc"] = True
        for o in seg:
            if o["dma"]:
                q = DMA_SEM[o["eng"]]
                i = self.dma_rr[q]
                self.dma_rr[q] = (i + 1) % self.DMA_POOL[q]
                sname = f"{q}{i}"
                o["prewait"] = (sname, self.sem_count[sname])
                self.sem_count[sname] += 16
                o["semval"] = (sname, self.sem_count[sname])
                o["inc"] = True
            elif o["inc"]:
                self.sem_count[o["eng"]] += 1
                o["semval"] = (o["eng"], self.sem_count[o["eng"]])
        for o in seg:
            waits = {}
            for d in o["deps"]:
                p = self.ops[d]
                if not p["dma"] and (p["blk"] < o["blk"] or
                                     (p["eng"] == o["eng"] and not p["hazard"])):
                    continue
                s, v = p["semval"]
                if waits.get(s, -1) < v:
                    waits[s] = v
            o["waits"] = waits
        seen = {}
        for o in seg:
            cur = seen.setdefault(o["eng"], {})
            pruned = {}
            for s, v in o["waits"].items():
                if cur.get(s, -1) < v:
                    pruned[s] = v
                    cur[s] = v
            o["waits"] = pruned
        by_eng = {}
        for o in seg:
            by_eng.setdefault(o["eng"], []).append(o)
        sems = self.sem
        with self.nc.Block(name) as blk:
            for eng, ops in by_eng.items():
                dec = getattr(blk, eng)

                def body(e, ops=ops):
                    for o in ops:
                        pw = o.get("prewait")
                        if pw is not None and pw[1] > 0:
                            e.wait_ge(sems[pw[0]], pw[1])
                        for s, v in o["waits"].items():
                            e.wait_ge(sems[s], v)
                        inst = o["fn"](e)
                        if o["inc"]:
                            s, v = o["semval"]
                            inst.then_inc(sems[s], 16 if o["dma"] else 1)
                dec(body)
        self.first_unemitted = len(self.ops)
        self.blk += 1

    def finalize(self):
        sems = self.sem
        counts = dict(self.sem_count)
        with self.nc.Block("fin") as blk:
            @blk.gpsimd
            def _(g):
                for q, n in self.DMA_POOL.items():
                    for i in range(n):
                        sname = f"{q}{i}"
                        if counts[sname]:
                            g.wait_ge(sems[sname], counts[sname])


def build_nc(debug=()):
    nc = bass.Bass("TRN2", target_bir_lowering=False)

    def dram(name, shape, dtype, kind=None):
        if kind is None:
            kind = "ExternalOutput" if name in debug else "Internal"
        return nc.dram_tensor(name, shape, dtype, kind=kind).ap()

    xt = dram("xt", [128, NC_D, T], F32, "ExternalInput")
    qw = dram("qw", [NC_D, 128, NC_D * 128], F32, "ExternalInput")
    kvw = dram("kvw", [8, 128, NC_D * 128], F32, "ExternalInput")
    cw = dram("cw", [NC_D, 128, NC_D * 128], F32, "ExternalInput")
    wg = dram("wg", [NC_D, 128, NE], F32, "ExternalInput")
    w1 = dram("w1", [NE + 2, NI, 128, NC_D * 128], F32, "ExternalInput")
    w3 = dram("w3", [NE + 2, NI, 128, NC_D * 128], F32, "ExternalInput")
    w2 = dram("w2", [NE + 2, NC_D, 128, NI * 128], F32, "ExternalInput")
    trig = dram("trig", [4, 128, T], F32, "ExternalInput")
    causal_d = dram("causal", [128, 128], F32, "ExternalInput")
    identf_d = dram("identf", [128, 128], F32, "ExternalInput")
    ones_r_d = dram("ones_r", [1, 128], F32, "ExternalInput")
    ones_c_d = dram("ones_c", [128, 1], F32, "ExternalInput")
    eye16_d = dram("eye16", [16, 16], F32, "ExternalInput")
    constv_d = dram("constv", [128, len(CONSTVALS)], F32, "ExternalInput")

    dbg_on = "dbg" in debug
    if dbg_on:
        dbg_d = dram("dbg", [24, 128, T], F32, "ExternalOutput")
    ao_d = dram("ao_d", [128, NC_D, T], BF16)
    out1_d = dram("out1_d", [128, NC_D, T], F32)
    xn2_d = dram("xn2_d", [128, NC_D, T], BF16)
    a3_d = dram("a3_d", [1, T], F32)
    b3_d = dram("b3_d", [1, T], F32)
    outy = dram("outy", [128, NC_D, T], F32, "ExternalOutput")

    with ExitStack() as octx:
        E = octx.enter_context
        b = Bld(nc, octx)

        c_ident = E(nc.sbuf_tensor("c_ident", [128, 128], F32))
        c_ones_r = E(nc.sbuf_tensor("c_ones_r", [1, 128], F32))
        c_ones_c = E(nc.sbuf_tensor("c_ones_c", [128, 1], F32))
        c_constv = E(nc.sbuf_tensor("c_constv", [128, len(CONSTVALS)], F32))
        b.dma("sync", c_ident[:, :], identf_d, reads=(), writes=("c_ident",))
        b.dma("sync", c_ones_r[:, :], ones_r_d, reads=(), writes=("c_ones_r",))
        b.dma("sync", c_ones_c[:, :], ones_c_d, reads=(), writes=("c_ones_c",))
        b.dma("sync", c_constv[:, :], constv_d, reads=(), writes=("constv",))
        for i, v in enumerate(CONSTVALS):
            nc.const_aps.register(F32, v, c_constv[:, i:i + 1])

        with ExitStack() as ctx:
            A = ctx.enter_context
            trq_c = A(nc.sbuf_tensor("trq_c", [128, T], F32))
            trq_s = A(nc.sbuf_tensor("trq_s", [128, T], F32))
            trk_c = A(nc.sbuf_tensor("trk_c", [128, T], F32))
            trk_s = A(nc.sbuf_tensor("trk_s", [128, T], F32))
            c_causal = A(nc.sbuf_tensor("c_causal", [128, 128], F32))
            c_identb = A(nc.sbuf_tensor("c_identb", [128, 128], BF16))
            xrot = A(nc.sbuf_tensor("xrot", [128, 4 * GSZ], F32))
            sqb = A(nc.sbuf_tensor("sqb", [128, 2 * GSZ], F32))
            rows = A(nc.sbuf_tensor("rows", [1, 5 * GSZ], F32))
            abca = A(nc.sbuf_tensor("abca", [128, GSZ], F32))
            abcb = A(nc.sbuf_tensor("abcb", [128, GSZ], F32))
            xn1 = A(nc.sbuf_tensor("xn1", [128, NC_D * GSZ], BF16))
            wrot = A(nc.sbuf_tensor("wrot", [128, 2 * NC_D * 128], BF16))
            Ksb = A(nc.sbuf_tensor("Ksb", [128, NKV * T], BF16))
            Vtok = A(nc.sbuf_tensor("Vtok", [128, NT_T * (NKV * 128)], BF16))
            Qsb = A(nc.sbuf_tensor("Qsb", [128, NH * GSZ], BF16))
            vfeat = A(nc.sbuf_tensor("vfeat", [128, GSZ], F32))
            swp = A(nc.sbuf_tensor("swp", [128, GSZ], F32))
            r1 = A(nc.sbuf_tensor("r1", [128, GSZ], F32))
            r2 = A(nc.sbuf_tensor("r2", [128, GSZ], F32))
            Pf = A(nc.sbuf_tensor("Pf", [128, 2 * T], BF16))
            P2 = A(nc.sbuf_tensor("P2", [128, 2 * T], BF16))
            sums4 = A(nc.sbuf_tensor("sums4", [128, 16], F32))
            tot = A(nc.sbuf_tensor("tot", [128, 4], F32))
            rec = A(nc.sbuf_tensor("rec", [128, 4], F32))
            PT = A(nc.sbuf_tensor("PT", [128, 2 * NT_T * 128], BF16))
            aosb = A(nc.sbuf_tensor("aosb", [128, NH * 128], BF16))
            bnd = A(nc.sbuf_tensor("bnd", [128, 1], F32))
            nrw = A(nc.sbuf_tensor("nrw", [1, 2 * GSZ], F32))
            nsc = A(nc.sbuf_tensor("nsc", [1, 16], F32))
            sqn = A(nc.sbuf_tensor("sqn", [128, GSZ], F32))

            pA0 = A(nc.psum_tensor("pA0", [128, 1024], BF16))
            pA1 = A(nc.psum_tensor("pA1", [128, 512], F32))
            pB0 = A(nc.psum_tensor("pB0", [128, 1024], BF16))
            pB1 = A(nc.psum_tensor("pB1", [128, 512], F32))
            pC = [A(nc.psum_tensor(f"pC{i}", [128, 512], F32)) for i in range(4)]

            for i, t in enumerate((trq_c, trq_s, trk_c, trk_s)):
                b.dma("sync", t[:, :], trig[i], reads=(), writes=(f"trig{i}",))
            b.dma("sync", c_causal[:, :], causal_d, reads=(), writes=("c_causal",))
            b.dma("gpsimd", c_identb[:, :], identf_d, reads=(), writes=("c_identb",))

            mrow = rows[:, 0:GSZ]
            exrow = rows[:, GSZ:2 * GSZ]
            varow = rows[:, 2 * GSZ:3 * GSZ]
            arow = rows[:, 3 * GSZ:4 * GSZ]
            brow = rows[:, 4 * GSZ:5 * GSZ]

            def ln_rows(rname, pm, pmn, pv, pvn, width=GSZ):
                b.ts("vector", mrow[:, 0:width], pm, INVD, None, ALU.mult, None,
                     reads=(pmn,), writes=(rname,))
                b.ts("vector", exrow[:, 0:width], pv, INVD, None, ALU.mult, None,
                     reads=(pvn,), writes=(rname,))
                b.tt("vector", varow[:, 0:width], mrow[:, 0:width], mrow[:, 0:width],
                     ALU.mult, reads=(rname,), writes=(rname,))
                b.tt("vector", varow[:, 0:width], exrow[:, 0:width], varow[:, 0:width],
                     ALU.subtract, reads=(rname,), writes=(rname,))
                b.act(varow[:, 0:width], varow[:, 0:width], AF.Ln, bias=EPS,
                      reads=(rname,), writes=(rname,))
                b.act(arow[:, 0:width], varow[:, 0:width], AF.Exp, scale=-0.5,
                      reads=(rname,), writes=(rname,))
                b.stt("vector", brow[:, 0:width], mrow[:, 0:width], -1.0,
                      arow[:, 0:width], ALU.mult, ALU.mult,
                      reads=(rname,), writes=(rname,))

            def bcast_ab(rname, pa_, pan, pb_, pbn, da, dan, db, dbn, width=GSZ):
                b.mm(pa_, c_ones_r[:, :], arow[:, 0:width], True, True,
                     reads=(rname, "c_ones_r"), writes=(pan,))
                b.mm(pb_, c_ones_r[:, :], brow[:, 0:width], True, True,
                     reads=(rname, "c_ones_r"), writes=(pbn,))
                b.copy("vector", da, pa_, reads=(pan,), writes=(dan,))
                b.copy("vector", db, pb_, reads=(pbn,), writes=(dbn,))

            def dump(slot, ap, res_reads, cast=False, prow=0, pn=128, w=T):
                if not dbg_on:
                    return
                eng = "gpsimd" if cast else "sync"
                dst = dbg_d[slot, prow:prow + pn, 0:w]
                b.op(eng, lambda e: e.dma_start(dst, ap, allow_slow_non_contiguous=True),
                     reads=res_reads, writes=(f"dbg{slot}_{prow}",), dma=True)

            for g in range(NG):
                T0 = g * GSZ
                for fc in range(NC_D):
                    slot = fc % 4
                    xs = xrot[:, slot * GSZ:(slot + 1) * GSZ]
                    b.dma("sync", xs, xt[:, fc:fc + 1, T0:T0 + GSZ],
                          reads=(f"xr{slot}",), writes=(f"xr{slot}",))
                    sq = sqb[:, (fc % 2) * GSZ:(fc % 2 + 1) * GSZ]
                    b.tt("vector", sq, xs, xs, ALU.mult,
                         reads=(f"xr{slot}",), writes=(f"sq{fc % 2}",))
                    b.mm(pA1[0:1, :], c_ones_c[:, :], xs, fc == 0, fc == NC_D - 1,
                         reads=(f"xr{slot}", "c_ones_c"), writes=("pA1",))
                    b.mm(pB1[0:1, :], c_ones_c[:, :], sq, fc == 0, fc == NC_D - 1,
                         reads=(f"sq{fc % 2}", "c_ones_c"), writes=("pB1",))
                ln_rows("rows", pA1[0:1, :], "pA1", pB1[0:1, :], "pB1")
                if g == 0:
                    dump(0, rows[:, 2 * GSZ:5 * GSZ], ("rows",), prow=0, pn=1, w=3 * GSZ)
                bcast_ab("rows", pA1[:, :], "pA1", pB1[:, :], "pB1",
                         abca[:, :], "abca", abcb[:, :], "abcb")
                for fc in range(NC_D):
                    slot = fc % 4
                    xs = xrot[:, slot * GSZ:(slot + 1) * GSZ]
                    b.dma("sync", xs, xt[:, fc:fc + 1, T0:T0 + GSZ],
                          reads=(f"xr{slot}",), writes=(f"xr{slot}",))
                    b.tt("vector", r1[:, :], xs, abca[:, :], ALU.mult,
                         reads=(f"xr{slot}", "abca"), writes=("r1",))
                    b.tt("vector", xn1[:, fc * GSZ:(fc + 1) * GSZ], r1[:, :], abcb[:, :],
                         ALU.add, reads=("r1", "abcb"), writes=(f"xn1_{fc}",))

                if g == 1:
                    dump(2, xn1[:, 0:4 * GSZ], tuple(f"xn1_{i}" for i in range(4)),
                         cast=True, w=4 * GSZ)
                for s in range(8):
                    ws = s % 2
                    wv = wrot[:, ws * NC_D * 128:(ws + 1) * NC_D * 128]
                    b.dma("gpsimd", wv, kvw[s], reads=(f"wr{ws}",), writes=(f"wr{ws}",))
                    pa = (pA1, pB1)[ws]
                    pan = ("pA1", "pB1")[ws]
                    for c in range(NC_D):
                        b.mm(pa[:, :], wv[:, c * 128:(c + 1) * 128],
                             xn1[:, c * GSZ:(c + 1) * GSZ], c == 0, c == NC_D - 1,
                             reads=(f"wr{ws}", f"xn1_{c}"), writes=(pan,))
                    if s < 4:
                        hk = s
                        if dbg_on and g == 1 and s == 0:
                            b.copy("vector", r2[:, :], pa[:, :], reads=(pan,), writes=("r2",))
                            dump(1, r2[:, 0:GSZ], ("r2",), w=GSZ)
                            dump(21, wv[:, :], (f"wr{ws}",), cast=True, w=NC_D * 128)
                        b.copy("vector", swp[0:64, :], pa[64:128, :], reads=(pan,), writes=("swp",))
                        b.copy("vector", swp[64:128, :], pa[0:64, :], reads=(pan,), writes=("swp",))
                        b.tt("vector", r1[:, :], pa[:, :], trk_c[:, T0:T0 + GSZ], ALU.mult,
                             reads=(pan, "trig2"), writes=("r1",))
                        b.tt("vector", r2[:, :], swp[:, :], trk_s[:, T0:T0 + GSZ], ALU.mult,
                             reads=("swp", "trig3"), writes=("r2",))
                        b.tt("vector", Ksb[:, hk * T + T0:hk * T + T0 + GSZ], r1[:, :],
                             r2[:, :], ALU.add, reads=("r1", "r2"), writes=(f"K_{hk}_{g}",))
                        if dbg_on and g == 1 and s == 0:
                            dump(16, swp[:, 0:GSZ], ("swp",), w=GSZ)
                            dump(17, r1[:, 0:GSZ], ("r1",), w=GSZ)
                            dump(18, r2[:, 0:GSZ], ("r2",), w=GSZ)
                            dump(19, trk_c[:, T0:T0 + GSZ], ("trig2",), w=GSZ)
                            dump(20, trk_s[:, T0:T0 + GSZ], ("trig3",), w=GSZ)
                    else:
                        dv = s - 4
                        b.copy("vector", vfeat[:, :], pa[:, :], reads=(pan,), writes=("vfeat",))
                        for sub in range(4):
                            b.transpose(pC[0][:, sub * 128:(sub + 1) * 128],
                                        vfeat[:, sub * 128:(sub + 1) * 128], c_ident[:, :],
                                        reads=("vfeat", "c_ident"), writes=("pC0",))
                        dst = Vtok[:, :].rearrange("p (tt dv) -> p tt dv", dv=NKV * 128)[
                            :, T0 // 128:T0 // 128 + 4, dv * 128:(dv + 1) * 128]
                        src = pC[0][:, :].rearrange("p (s c) -> p s c", c=128)
                        b.copy("vector", dst, src, reads=("pC0",), writes=(f"V_{dv}_{g}",))

                for h in range(NH):
                    ws = h % 2
                    wv = wrot[:, ws * NC_D * 128:(ws + 1) * NC_D * 128]
                    b.dma("gpsimd", wv, qw[h], reads=(f"wr{ws}",), writes=(f"wr{ws}",))
                    pa = (pA1, pB1)[ws]
                    pan = ("pA1", "pB1")[ws]
                    for c in range(NC_D):
                        b.mm(pa[:, :], wv[:, c * 128:(c + 1) * 128],
                             xn1[:, c * GSZ:(c + 1) * GSZ], c == 0, c == NC_D - 1,
                             reads=(f"wr{ws}", f"xn1_{c}"), writes=(pan,))
                    b.copy("vector", swp[0:64, :], pa[64:128, :], reads=(pan,), writes=("swp",))
                    b.copy("vector", swp[64:128, :], pa[0:64, :], reads=(pan,), writes=("swp",))
                    b.tt("vector", r1[:, :], pa[:, :], trq_c[:, T0:T0 + GSZ], ALU.mult,
                         reads=(pan, "trig0"), writes=("r1",))
                    b.tt("vector", r2[:, :], swp[:, :], trq_s[:, T0:T0 + GSZ], ALU.mult,
                         reads=("swp", "trig1"), writes=("r2",))
                    b.tt("vector", Qsb[:, h * GSZ:(h + 1) * GSZ], r1[:, :], r2[:, :],
                         ALU.add, reads=("r1", "r2"), writes=(f"Q_{h}",))

                if g == 1:
                    dump(3, Ksb[:, GSZ:2 * GSZ], ("K_0_1",), cast=True, w=GSZ)
                    dump(4, Qsb[:, 0:GSZ], ("Q_0",), cast=True, w=GSZ)
                    dump(5, Vtok[:, 0:4 * 512], tuple(f"V_{dv}_0" for dv in range(4)),
                         cast=True, w=4 * 512)
                for h in range(NH):
                    b.tt("vector", sqn[:, :], Qsb[:, h * GSZ:(h + 1) * GSZ],
                         Qsb[:, h * GSZ:(h + 1) * GSZ], ALU.mult,
                         reads=(f"Q_{h}",), writes=("sqn",))
                    b.mm(pA1[0:1, :], c_ones_c[:, :], sqn[:, :], h == 0, h == NH - 1,
                         reads=("sqn", "c_ones_c"), writes=("pA1",))
                for hk in range(NKV):
                    b.tt("vector", sqn[:, :], Ksb[:, hk * T + T0:hk * T + T0 + GSZ],
                         Ksb[:, hk * T + T0:hk * T + T0 + GSZ], ALU.mult,
                         reads=(f"K_{hk}_{g}",), writes=("sqn",))
                    b.mm(pB1[0:1, :], c_ones_c[:, :], sqn[:, :], hk == 0, hk == NKV - 1,
                         reads=("sqn", "c_ones_c"), writes=("pB1",))
                b.copy("vector", nrw[:, 0:GSZ], pA1[0:1, :], reads=("pA1",), writes=("nrw",))
                b.copy("vector", nrw[:, GSZ:2 * GSZ], pB1[0:1, :], reads=("pB1",), writes=("nrw",))
                b.red(nsc[:, 0:1], nrw[:, 0:GSZ], ALU.max, reads=("nrw",), writes=("nsc",))
                b.red(nsc[:, 1:2], nrw[:, GSZ:2 * GSZ], ALU.max, reads=("nrw",), writes=("nsc",))
                if g == 0:
                    b.copy("vector", nsc[:, 2:3], nsc[:, 1:2], reads=("nsc",),
                           writes=("nsc",), hazard=True)
                else:
                    b.tt("vector", nsc[:, 2:3], nsc[:, 2:3], nsc[:, 1:2], ALU.max,
                         reads=("nsc",), writes=("nsc",), hazard=True)
                b.act(nsc[:, 3:4], nsc[:, 0:1], AF.Ln, reads=("nsc",), writes=("nsc",))
                b.act(nsc[:, 4:5], nsc[:, 2:3], AF.Ln, reads=("nsc",), writes=("nsc",))
                b.tt("vector", nsc[:, 5:6], nsc[:, 3:4], nsc[:, 4:5], ALU.add,
                     reads=("nsc",), writes=("nsc",), hazard=True)
                b.act(nsc[:, 6:7], nsc[:, 5:6], AF.Exp, scale=0.5, reads=("nsc",), writes=("nsc",))
                b.ts("vector", nsc[:, 7:8], nsc[:, 6:7], 1.0, 60.0, ALU.add, ALU.min,
                     reads=("nsc",), writes=("nsc",), hazard=True)
                b.ts("vector", nsc[:, 8:9], nsc[:, 7:8], -1.0, None, ALU.mult, None,
                     reads=("nsc",), writes=("nsc",), hazard=True)
                b.mm(pA1[:, 0:1], c_ones_r[:, :], nsc[:, 8:9], True, True,
                     reads=("nsc", "c_ones_r"), writes=("pA1",))
                b.copy("vector", bnd[:, :], pA1[:, 0:1], reads=("pA1",), writes=("bnd",))
                if dbg_on:
                    dump(22, bnd[:, 0:1], ("bnd",), prow=0, pn=128, w=1)
                    dump(23, nsc[:, 0:16], ("nsc",), prow=0, pn=1, w=16)

                B6 = [("pC0", pC[0]), ("pC1", pC[1]), ("pC2", pC[2]), ("pC3", pC[3])]
                PTB = [("pA0", pA0), ("pB0", pB0)]
                PVB = [("pA1", pA1), ("pB1", pB1)]
                for qq in range(4):
                    jj = g * 4 + qq
                    kmax = (jj + 1) * 128
                    nbank = (kmax + 511) // 512
                    tq0 = jj * 128
                    nkt = jj + 1
                    if nbank <= 2:
                        nsets, wslot = 4, 1024
                    else:
                        nsets, wslot = 2, 2048
                    cur = [0]

                    def part1(h, jj=jj, kmax=kmax, nbank=nbank, g=g, qq=qq,
                              nsets=nsets, wslot=wslot, cur=cur):
                        pi = h % nsets
                        hk = h // 4
                        banks = [B6[(cur[0] + i) % 4] for i in range(nbank)]
                        cur[0] = (cur[0] + nbank) % 4
                        kreads = tuple(f"K_{hk}_{gg}" for gg in range(g + 1))
                        for kb in range(nbank):
                            w = min(512, kmax - kb * 512)
                            bn, bt = banks[kb]
                            b.mm(bt[:, 0:w],
                                 Qsb[:, h * GSZ + qq * 128:h * GSZ + (qq + 1) * 128],
                                 Ksb[:, hk * T + kb * 512:hk * T + kb * 512 + w],
                                 True, True,
                                 reads=(f"Q_{h}", "bnd") + kreads, writes=(bn,))
                        kbd, off = divmod(kmax - 128, 512)
                        bn, bt = banks[kbd]
                        b.tt("vector", bt[:, off:off + 128], bt[:, off:off + 128],
                             c_causal[:, :], ALU.add,
                             reads=(bn, "c_causal"), writes=(bn,))
                        for kb in range(nbank):
                            w = min(512, kmax - kb * 512)
                            bn, bt = banks[kb]
                            b.act(Pf[:, pi * wslot + kb * 512:pi * wslot + kb * 512 + w],
                                  bt[:, 0:w], AF.Exp, bias=bnd[:, :],
                                  accum_out=sums4[:, pi * 4 + kb:pi * 4 + kb + 1],
                                  reads=(bn, "bnd"), writes=(f"Pf{pi}", f"sums4_{pi}"))
                        recs = rec[:, pi:pi + 1]
                        if nbank == 1:
                            b.recip(recs, sums4[:, pi * 4:pi * 4 + 1],
                                    reads=(f"sums4_{pi}",), writes=(f"rec{pi}",))
                        else:
                            tots = tot[:, pi:pi + 1]
                            b.tt("vector", tots, sums4[:, pi * 4:pi * 4 + 1],
                                 sums4[:, pi * 4 + 1:pi * 4 + 2], ALU.add,
                                 reads=(f"sums4_{pi}",), writes=(f"tot{pi}",), hazard=True)
                            for kb in range(2, nbank):
                                b.tt("vector", tots, tots, sums4[:, pi * 4 + kb:pi * 4 + kb + 1],
                                     ALU.add, reads=(f"sums4_{pi}", f"tot{pi}"),
                                     writes=(f"tot{pi}",), hazard=True)
                            b.recip(recs, tots, reads=(f"tot{pi}",), writes=(f"rec{pi}",))
                        b.ts("vector", P2[:, pi * wslot:pi * wslot + kmax],
                             Pf[:, pi * wslot:pi * wslot + kmax], recs, None, ALU.mult,
                             None, reads=(f"Pf{pi}", f"rec{pi}"), writes=(f"P2{pi}",))

                    def part2(h, jj=jj, kmax=kmax, g=g, nkt=nkt, nsets=nsets, wslot=wslot):
                        pi = h % nsets
                        hk = h // 4
                        ptn, ptb = PTB[h % 2]
                        pvn, pvb = PVB[h % 2]
                        for kt0 in range(0, nkt, 8):
                            kn = min(8, nkt - kt0)
                            for k in range(kn):
                                b.transpose(ptb[:, k * 128:(k + 1) * 128],
                                            P2[:, pi * wslot + (kt0 + k) * 128:
                                               pi * wslot + (kt0 + k + 1) * 128],
                                            c_identb[:, :],
                                            reads=(f"P2{pi}", "c_identb"), writes=(ptn,))
                            b.copy("vector",
                                   PT[:, pi * wslot + kt0 * 128:pi * wslot + (kt0 + kn) * 128],
                                   ptb[:, 0:kn * 128], reads=(ptn,), writes=(f"PT{pi}",))
                        vreads = tuple(f"V_{dv}_{gg}" for dv in range(4) for gg in range(g + 1))
                        for kt in range(nkt):
                            b.mm(pvb[:, 0:128],
                                 Vtok[:, kt * (NKV * 128) + hk * 128:
                                      kt * (NKV * 128) + (hk + 1) * 128],
                                 PT[:, pi * wslot + kt * 128:pi * wslot + (kt + 1) * 128],
                                 kt == 0, kt == nkt - 1,
                                 reads=(f"PT{pi}",) + vreads, writes=(pvn,))
                        b.copy("vector", aosb[:, h * 128:(h + 1) * 128], pvb[:, 0:128],
                               reads=(pvn,), writes=(f"ao_h{h}",))

                    defer = 2 if nsets >= 4 else 1
                    for h in range(NH):
                        part1(h)
                        if h >= defer:
                            part2(h - defer)
                    for h in range(NH - defer, NH):
                        part2(h)
                    b.dma("sync", ao_d[:, :, tq0:tq0 + 128],
                          aosb[:, :].rearrange("p (h t) -> p h t", t=128),
                          reads=tuple(f"ao_h{hh}" for hh in range(NH)),
                          writes=("ao_dram",))
            b.flush("phaseA")

        with ExitStack() as ctx:
            A = ctx.enter_context
            cwb = A(nc.sbuf_tensor("cwb", [128, NC_D * NC_D * 128], BF16))
            aot = A(nc.sbuf_tensor("aot", [128, 2 * NC_D * GSZ], BF16))
            ap = A(nc.sbuf_tensor("ap", [128, 2 * NC_D * GSZ], F32))
            xrot = A(nc.sbuf_tensor("xrotB", [128, 4 * GSZ], F32))
            sqb = A(nc.sbuf_tensor("sqbB", [128, 2 * GSZ], F32))
            rows = A(nc.sbuf_tensor("rowsB", [1, 5 * GSZ], F32))
            abca = A(nc.sbuf_tensor("abcaB", [128, GSZ], F32))
            abcb = A(nc.sbuf_tensor("abcbB", [128, GSZ], F32))
            r1 = A(nc.sbuf_tensor("r1B", [128, GSZ], F32))
            xnb = A(nc.sbuf_tensor("xnb", [128, 2 * GSZ], BF16))
            xpl = A(nc.sbuf_tensor("xpl", [128, 2 * GSZ], F32))

            pY0 = A(nc.psum_tensor("pY0", [128, 512], F32))
            pY1 = A(nc.psum_tensor("pY1", [128, 512], F32))
            pS0 = A(nc.psum_tensor("pS0", [128, 512], F32))
            pS1 = A(nc.psum_tensor("pS1", [128, 512], F32))

            mrow = rows[:, 0:GSZ]
            exrow = rows[:, GSZ:2 * GSZ]
            varow = rows[:, 2 * GSZ:3 * GSZ]
            arow = rows[:, 3 * GSZ:4 * GSZ]
            brow = rows[:, 4 * GSZ:5 * GSZ]

            def ln_rowsB():
                b.ts("vector", mrow, pS0[0:1, :], INVD, None, ALU.mult, None,
                     reads=("pS0",), writes=("rowsB",))
                b.ts("vector", exrow, pS1[0:1, :], INVD, None, ALU.mult, None,
                     reads=("pS1",), writes=("rowsB",))
                b.tt("vector", varow, mrow, mrow, ALU.mult, reads=("rowsB",), writes=("rowsB",))
                b.tt("vector", varow, exrow, varow, ALU.subtract, reads=("rowsB",), writes=("rowsB",))
                b.act(varow, varow, AF.Ln, bias=EPS, reads=("rowsB",), writes=("rowsB",))
                b.act(arow, varow, AF.Exp, scale=-0.5, reads=("rowsB",), writes=("rowsB",))
                b.stt("vector", brow, mrow, -1.0, arow, ALU.mult, ALU.mult,
                      reads=("rowsB",), writes=("rowsB",))

            def bcast_abB():
                b.mm(pS0[:, :], c_ones_r[:, :], arow, True, True,
                     reads=("rowsB", "c_ones_r"), writes=("pS0",))
                b.mm(pS1[:, :], c_ones_r[:, :], brow, True, True,
                     reads=("rowsB", "c_ones_r"), writes=("pS1",))
                b.copy("vector", abca[:, :], pS0[:, :], reads=("pS0",), writes=("abcaB",))
                b.copy("vector", abcb[:, :], pS1[:, :], reads=("pS1",), writes=("abcbB",))

            for sl in range(NC_D):
                b.dma("gpsimd", cwb[:, sl * NC_D * 128:(sl + 1) * NC_D * 128], cw[sl],
                      reads=(), writes=(f"cwb{sl}",))

            W_B = NC_D * GSZ

            def projB(g):
                gi = g % 2
                T0 = g * GSZ
                aotv = aot[:, gi * W_B:(gi + 1) * W_B]
                apv = ap[:, gi * W_B:(gi + 1) * W_B]
                b.dma("sync", aotv, ao_d[:, :, T0:T0 + GSZ],
                      reads=("ao_dram",), writes=(f"aot{gi}",))
                for sl in range(NC_D):
                    py = (pY0, pY1)[sl % 2]
                    pyn = f"pY{sl % 2}"
                    for c in range(NC_D):
                        b.mm(py[:, :], cwb[:, (sl * NC_D + c) * 128:(sl * NC_D + c + 1) * 128],
                             aotv[:, c * GSZ:(c + 1) * GSZ], c == 0, c == NC_D - 1,
                             reads=(f"cwb{sl}", f"aot{gi}"), writes=(pyn,))
                    b.copy("vector", apv[:, sl * GSZ:(sl + 1) * GSZ], py[:, :],
                           reads=(pyn,), writes=(f"ap{gi}_{sl}",))

            def lnB(g):
                gi = g % 2
                T0 = g * GSZ
                apv = ap[:, gi * W_B:(gi + 1) * W_B]
                o1v = apv
                for fc in range(NC_D):
                    eng = "vector"
                    aps = apv[:, fc * GSZ:(fc + 1) * GSZ]
                    sq = sqb[:, (fc % 2) * GSZ:(fc % 2 + 1) * GSZ]
                    b.tt(eng, sq, aps, aps, ALU.mult,
                         reads=(f"ap{gi}_{fc}",), writes=(f"sqB{fc % 2}",))
                    b.mm(pS0[0:1, :], c_ones_c[:, :], aps, fc == 0, fc == NC_D - 1,
                         reads=(f"ap{gi}_{fc}", "c_ones_c"), writes=("pS0",))
                    b.mm(pS1[0:1, :], c_ones_c[:, :], sq, fc == 0, fc == NC_D - 1,
                         reads=(f"sqB{fc % 2}", "c_ones_c"), writes=("pS1",))
                ln_rowsB()
                bcast_abB()
                for fc in range(NC_D):
                    eng = "vector"
                    slot = fc % 4
                    rsl = fc % 2
                    xs = xrot[:, slot * GSZ:(slot + 1) * GSZ]
                    b.dma("sync", xs, xt[:, fc:fc + 1, T0:T0 + GSZ],
                          reads=(f"xB{slot}",), writes=(f"xB{slot}",))
                    o1s = o1v[:, fc * GSZ:(fc + 1) * GSZ]
                    rr = xpl[:, rsl * GSZ:(rsl + 1) * GSZ]
                    b.tt(eng, rr, apv[:, fc * GSZ:(fc + 1) * GSZ], abca[:, :],
                         ALU.mult, reads=(f"ap{gi}_{fc}", "abcaB"), writes=(f"xpl{rsl}",))
                    b.tt(eng, rr, rr, abcb[:, :], ALU.add,
                         reads=(f"xpl{rsl}", "abcbB"), writes=(f"xpl{rsl}",))
                    b.tt(eng, o1s, rr, xs, ALU.add,
                         reads=(f"xpl{rsl}", f"xB{slot}"), writes=(f"o1{gi}_{fc}", f"ap{gi}_{fc}"))
                    b.dma("sync", out1_d[:, fc:fc + 1, T0:T0 + GSZ], o1s,
                          reads=(f"o1{gi}_{fc}",), writes=("out1_dram",))
                for fc in range(NC_D):
                    eng = "vector"
                    o1s = o1v[:, fc * GSZ:(fc + 1) * GSZ]
                    sq = sqb[:, (fc % 2) * GSZ:(fc % 2 + 1) * GSZ]
                    b.tt(eng, sq, o1s, o1s, ALU.mult,
                         reads=(f"o1{gi}_{fc}",), writes=(f"sqB{fc % 2}",))
                    b.mm(pS0[0:1, :], c_ones_c[:, :], o1s, fc == 0, fc == NC_D - 1,
                         reads=(f"o1{gi}_{fc}", "c_ones_c"), writes=("pS0",))
                    b.mm(pS1[0:1, :], c_ones_c[:, :], sq, fc == 0, fc == NC_D - 1,
                         reads=(f"sqB{fc % 2}", "c_ones_c"), writes=("pS1",))
                ln_rowsB()
                b.dma("sync", a3_d[:, T0:T0 + GSZ], arow, reads=("rowsB",), writes=("a3_dram",))
                b.dma("sync", b3_d[:, T0:T0 + GSZ], brow, reads=("rowsB",), writes=("b3_dram",))
                bcast_abB()
                for fc in range(NC_D):
                    eng = "gpsimd"
                    xns = xnb[:, (fc % 2) * GSZ:(fc % 2 + 1) * GSZ]
                    xpv = xpl[:, (fc % 2) * GSZ:(fc % 2 + 1) * GSZ]
                    b.tt(eng, xpv, o1v[:, fc * GSZ:(fc + 1) * GSZ], abca[:, :],
                         ALU.mult, reads=(f"o1{gi}_{fc}", "abcaB"), writes=(f"xpl{fc % 2}",))
                    b.tt(eng, xns, xpv, abcb[:, :],
                         ALU.add, reads=(f"xpl{fc % 2}", "abcbB"), writes=(f"xnb{fc % 2}",))
                    b.dma("sync", xn2_d[:, fc:fc + 1, T0:T0 + GSZ], xns,
                          reads=(f"xnb{fc % 2}",), writes=("xn2_dram",))

            projB(0)
            for g in range(NG):
                if g + 1 < NG:
                    projB(g + 1)
                lnB(g)
            b.flush("phaseB")

        with ExitStack() as ctx:
            A = ctx.enter_context
            wgt = A(nc.sbuf_tensor("wgt", [128, NC_D * NE], F32))
            c_eye16 = A(nc.sbuf_tensor("c_eye16", [16, 16], F32))
            xn2c = A(nc.sbuf_tensor("xn2c", [128, NC_D * CHUNK], BF16))
            yac = A(nc.sbuf_tensor("yac", [128, NC_D * CHUNK], F32))
            hsb = A(nc.sbuf_tensor("hsb", [128, NI * CHUNK], BF16))
            w1b = A(nc.sbuf_tensor("w1b", [128, 2 * NC_D * 128], BF16))
            w3b = A(nc.sbuf_tensor("w3b", [128, 2 * NC_D * 128], BF16))
            w2b = A(nc.sbuf_tensor("w2b", [128, 3 * NI * 128], BF16))
            orot = A(nc.sbuf_tensor("orot", [128, 2 * 512], F32))
            lgt = A(nc.sbuf_tensor("lgt", [16, CHUNK], F32))
            ltok = A(nc.sbuf_tensor("ltok", [128, 8 * 16], F32))
            scr = A(nc.sbuf_tensor("scr", [128, 8 * 16], F32))
            gsb = A(nc.sbuf_tensor("gsb", [128, 8 * 4], F32))
            wrk = A(nc.sbuf_tensor("wrk", [128, 8 * 16], F32))
            sel = A(nc.sbuf_tensor("sel", [128, 8 * 16], F32))
            cmb = A(nc.sbuf_tensor("cmb", [128, 8 * 16], F32))
            ctr = A(nc.sbuf_tensor("ctr", [16, 2 * CHUNK], F32))
            csb = A(nc.sbuf_tensor("csb", [128, CHUNK], F32))
            crow = A(nc.sbuf_tensor("crow", [1, CHUNK], F32))
            m1c = A(nc.sbuf_tensor("m1c", [128, 1], F32))
            totC = A(nc.sbuf_tensor("totC", [128, 1], F32))
            recC = A(nc.sbuf_tensor("recC", [128, 1], F32))
            wgbc = A(nc.sbuf_tensor("wgbc", [128, 16], F32))
            a3c = A(nc.sbuf_tensor("a3c", [128, 8], F32))
            b3c = A(nc.sbuf_tensor("b3c", [128, 8], F32))
            wgrow = A(nc.sbuf_tensor("wgrow", [1, 16], F32))
            a3w = A(nc.sbuf_tensor("a3w", [1, CHUNK], F32))
            b3w = A(nc.sbuf_tensor("b3w", [1, CHUNK], F32))
            ssc = A(nc.sbuf_tensor("ssc", [128, 512], F32))
            h1c = A(nc.sbuf_tensor("h1c", [128, 512], F32))
            rows = A(nc.sbuf_tensor("rowsC", [1, 5 * 512], F32))
            abca = A(nc.sbuf_tensor("abcaC", [128, 512], F32))
            abcb = A(nc.sbuf_tensor("abcbC", [128, 512], F32))
            r1 = A(nc.sbuf_tensor("r1C", [128, 512], F32))
            o2 = A(nc.sbuf_tensor("o2", [128, 2 * 512], F32))
            sqc = A(nc.sbuf_tensor("sqc", [128, 2 * 512], F32))

            pA = [A(nc.psum_tensor(f"qA{i}", [128, 512], F32)) for i in range(2)]
            pB = [A(nc.psum_tensor(f"qB{i}", [128, 512], F32)) for i in range(2)]
            pY = [A(nc.psum_tensor(f"qY{i}", [128, 512], F32)) for i in range(2)]
            pR = A(nc.psum_tensor("qR", [128, 512], F32))
            pS = A(nc.psum_tensor("qS", [128, 512], F32))

            for c in range(NC_D):
                b.dma("sync", wgt[:, c * NE:(c + 1) * NE], wg[c], reads=(), writes=("wgt",))
            b.dma("sync", c_eye16[:, :], eye16_d, reads=(), writes=("c_eye16",))
            for c in range(NC_D):
                b.mm(pR[0:1, 0:NE], c_ones_c[:, :], wgt[:, c * NE:(c + 1) * NE],
                     c == 0, c == NC_D - 1, reads=("wgt", "c_ones_c"), writes=("qR",))
            b.copy("vector", wgrow[:, :], pR[0:1, 0:NE], reads=("qR",), writes=("wgrow",), hazard=True)
            b.mm(pR[:, 0:16], c_ones_r[:, :], wgrow[:, :], True, True,
                 reads=("wgrow", "c_ones_r"), writes=("qR",))
            b.copy("vector", wgbc[:, :], pR[:, 0:16], reads=("qR",), writes=("wgbc",), hazard=True)

            mrow = rows[:, 0:512]
            exrow = rows[:, 512:2 * 512]
            varow = rows[:, 2 * 512:3 * 512]
            arow = rows[:, 3 * 512:4 * 512]
            brow = rows[:, 4 * 512:5 * 512]

            def routerC(ch):
                C0 = ch * CHUNK
                b.dma("sync", a3w[:, :], a3_d[:, C0:C0 + CHUNK],
                      reads=("a3_dram",), writes=("a3w",))
                b.dma("sync", b3w[:, :], b3_d[:, C0:C0 + CHUNK],
                      reads=("b3_dram",), writes=("b3w",))
                for half in range(2):
                    for c in range(NC_D):
                        slot = c % 2
                        xs = orot[:, slot * 512:(slot + 1) * 512]
                        b.dma("sync", xs,
                              out1_d[:, c:c + 1, C0 + half * 512:C0 + (half + 1) * 512],
                              reads=(f"or{slot}",), writes=(f"or{slot}",))
                        b.mm(pR[0:NE, 0:512], wgt[:, c * NE:(c + 1) * NE], xs,
                             c == 0, c == NC_D - 1,
                             reads=(f"or{slot}", "wgt"), writes=("qR",))
                    b.copy("vector", lgt[:, half * 512:(half + 1) * 512], pR[0:NE, 0:512],
                           reads=("qR",), writes=("lgt",))
                for tt8 in range(8):
                    b.transpose(pS[:, 0:16], lgt[:, tt8 * 128:(tt8 + 1) * 128], c_eye16[:, :],
                                reads=("lgt", "c_eye16"), writes=("qS",))
                    b.copy("vector", ltok[:, tt8 * 16:(tt8 + 1) * 16], pS[:, 0:16],
                           reads=("qS",), writes=("ltok",))
                    b.mm(pS[:, 0:1], a3w[:, tt8 * 128:(tt8 + 1) * 128],
                         c_ones_r[0:1, 0:1], True, True,
                         reads=("a3w", "c_ones_r"), writes=("qS",))
                    b.copy("vector", a3c[:, tt8:tt8 + 1], pS[:, 0:1], reads=("qS",), writes=("a3c",), hazard=True)
                    b.mm(pS[:, 0:1], b3w[:, tt8 * 128:(tt8 + 1) * 128],
                         c_ones_r[0:1, 0:1], True, True,
                         reads=("b3w", "c_ones_r"), writes=("qS",))
                    b.copy("vector", b3c[:, tt8:tt8 + 1], pS[:, 0:1], reads=("qS",), writes=("b3c",), hazard=True)
                for tt8 in range(8):
                    lt = ltok[:, tt8 * 16:(tt8 + 1) * 16]
                    st = scr[:, tt8 * 16:(tt8 + 1) * 16]
                    b.ts("vector", lt, lt, a3c[:, tt8:tt8 + 1], None, ALU.mult, None,
                         reads=("ltok", "a3c"), writes=("ltok",), hazard=True)
                    b.stt("vector", lt, wgbc[:, :], b3c[:, tt8:tt8 + 1], lt,
                          ALU.mult, ALU.add, reads=("ltok", "wgbc", "b3c"),
                          writes=("ltok",), hazard=True)
                    b.red(m1c[:, :], lt, ALU.max, reads=("ltok",), writes=("m1c",), negate=True)
                    b.act(st, lt, AF.Exp, bias=m1c[:, :], accum_out=totC[:, :],
                          reads=("ltok", "m1c"), writes=("scr", "totC"))
                    b.recip(recC[:, :], totC[:, :], reads=("totC",), writes=("recC",))
                    b.ts("vector", st, st, recC[:, :], None, ALU.mult, None,
                         reads=("scr", "recC"), writes=("scr",), hazard=True)
                    gs = gsb[:, tt8 * 4:(tt8 + 1) * 4]
                    b.red(gs, st.rearrange("p (g e) -> p g e", e=4), ALU.max,
                          reads=("scr",), writes=("gsb",))
                    b.red(m1c[:, :], gs, ALU.max, reads=("gsb",), writes=("m1c",))
                    eq1 = wrk[:, tt8 * 16:tt8 * 16 + 4]
                    b.ts("vector", eq1, gs, m1c[:, :], None, ALU.is_ge, None,
                         reads=("gsb", "m1c"), writes=("wrk",), hazard=True)
                    b.stt("vector", gs, eq1, -BIG, gs, ALU.mult, ALU.add,
                          reads=("gsb", "wrk"), writes=("gsb",), hazard=True)
                    b.red(m1c[:, :], gs, ALU.max, reads=("gsb",), writes=("m1c",))
                    b.ts("vector", gs, gs, m1c[:, :], None, ALU.is_ge, None,
                         reads=("gsb", "m1c"), writes=("gsb",), hazard=True)
                    b.tt("vector", gs, gs, eq1, ALU.add, reads=("gsb", "wrk"),
                         writes=("gsb",), hazard=True)
                    b.ts("vector", gs, gs, 1.0, BIG, ALU.subtract, ALU.mult,
                         reads=("gsb",), writes=("gsb",), hazard=True)
                    wt = wrk[:, tt8 * 16:(tt8 + 1) * 16]
                    for j in range(4):
                        b.copy("vector",
                               wt.rearrange("p (g e) -> p g e", e=4)[:, :, j:j + 1],
                               gs, reads=("gsb",), writes=("wrk",), hazard=True)
                    b.tt("vector", wt, wt, st, ALU.add, reads=("wrk", "scr"),
                         writes=("wrk",), hazard=True)
                    se = sel[:, tt8 * 16:(tt8 + 1) * 16]
                    eq = cmb[:, tt8 * 16:(tt8 + 1) * 16]
                    for it in range(TOPK):
                        b.red(m1c[:, :], wt, ALU.max, reads=("wrk",), writes=("m1c",))
                        b.ts("vector", eq, wt, m1c[:, :], None, ALU.is_ge, None,
                             reads=("wrk", "m1c"), writes=("cmb",), hazard=True)
                        if it == 0:
                            b.copy("vector", se, eq, reads=("cmb",), writes=("sel",), hazard=True)
                        else:
                            b.tt("vector", se, se, eq, ALU.add, reads=("sel", "cmb"),
                                 writes=("sel",), hazard=True)
                        if it < TOPK - 1:
                            b.stt("vector", wt, eq, -BIG, wt, ALU.mult, ALU.add,
                                  reads=("cmb", "wrk"), writes=("wrk",), hazard=True)
                    b.tt("vector", eq, st, se, ALU.mult, reads=("scr", "sel"),
                         writes=("cmb",), hazard=True)
                for tt8 in range(8):
                    b.transpose(pS[0:16, 0:128], cmb[:, tt8 * 16:(tt8 + 1) * 16],
                                c_ident[:, :], reads=("cmb", "c_ident"), writes=("qS",))
                    b.copy("vector", ctr[:, ch * CHUNK + tt8 * 128:ch * CHUNK + (tt8 + 1) * 128],
                           pS[0:16, 0:128], reads=("qS",), writes=(f"ctr{ch}",))

            def ln4(ch):
                C0 = ch * CHUNK
                for half in range(2):
                    H0 = half * 512
                    for fc in range(NC_D):
                        ys = yac[:, fc * CHUNK + H0:fc * CHUNK + H0 + 512]
                        sq = sqc[:, (fc % 2) * 512:(fc % 2 + 1) * 512]
                        b.tt("vector", sq, ys, ys, ALU.mult,
                             reads=(f"y_{fc}",), writes=(f"sqC{fc % 2}",))
                        b.mm(pS[0:1, :], c_ones_c[:, :], ys, fc == 0, fc == NC_D - 1,
                             reads=(f"y_{fc}", "c_ones_c"), writes=("qS",))
                        b.mm(pR[0:1, :], c_ones_c[:, :], sq, fc == 0, fc == NC_D - 1,
                             reads=(f"sqC{fc % 2}", "c_ones_c"), writes=("qR",))
                    b.ts("vector", mrow, pS[0:1, :], INVD, None, ALU.mult, None,
                         reads=("qS",), writes=("rowsC",))
                    b.ts("vector", exrow, pR[0:1, :], INVD, None, ALU.mult, None,
                         reads=("qR",), writes=("rowsC",))
                    b.tt("vector", varow, mrow, mrow, ALU.mult, reads=("rowsC",), writes=("rowsC",))
                    b.tt("vector", varow, exrow, varow, ALU.subtract, reads=("rowsC",), writes=("rowsC",))
                    b.act(varow, varow, AF.Ln, bias=EPS, reads=("rowsC",), writes=("rowsC",))
                    b.act(arow, varow, AF.Exp, scale=-0.5, reads=("rowsC",), writes=("rowsC",))
                    b.stt("vector", brow, mrow, -1.0, arow, ALU.mult, ALU.mult,
                          reads=("rowsC",), writes=("rowsC",))
                    b.mm(pS[:, :], c_ones_r[:, :], arow, True, True,
                         reads=("rowsC", "c_ones_r"), writes=("qS",))
                    b.mm(pR[:, :], c_ones_r[:, :], brow, True, True,
                         reads=("rowsC", "c_ones_r"), writes=("qR",))
                    b.copy("vector", abca[:, :], pS[:, :], reads=("qS",), writes=("abcaC",))
                    b.copy("vector", abcb[:, :], pR[:, :], reads=("qR",), writes=("abcbC",))
                    for fc in range(NC_D):
                        slot = fc % 2
                        ys = yac[:, fc * CHUNK + H0:fc * CHUNK + H0 + 512]
                        o2s = o2[:, slot * 512:(slot + 1) * 512]
                        xs = orot[:, slot * 512:(slot + 1) * 512]
                        b.dma("sync", xs, out1_d[:, fc:fc + 1, C0 + H0:C0 + H0 + 512],
                              reads=(f"or{slot}",), writes=(f"or{slot}",))
                        b.tt("vector", r1[:, :], ys, abca[:, :], ALU.mult,
                             reads=(f"y_{fc}", "abcaC"), writes=("r1C",))
                        b.tt("vector", r1[:, :], r1[:, :], abcb[:, :], ALU.add,
                             reads=("r1C", "abcbC"), writes=("r1C",))
                        b.tt("vector", o2s, r1[:, :], xs, ALU.add,
                             reads=("r1C", f"or{slot}"), writes=(f"o2_{slot}",))
                        b.dma("sync", outy[:, fc:fc + 1, C0 + H0:C0 + H0 + 512], o2s,
                              reads=(f"o2_{slot}",), writes=("outy_dram",))

            routerC(0)
            for ch in range(NCH):
                C0 = ch * CHUNK
                b.dma("sync", xn2c[:, :], xn2_d[:, :, C0:C0 + CHUNK],
                      reads=("xn2_dram",), writes=("xn2c",))
                for e in range(NE + 2):
                    if ch == 0 and e == 5:
                        routerC(1)
                    routed = e < NE
                    if routed:
                        for half in range(2):
                            b.mm(pR[0:1, :], c_eye16[:, e:e + 1],
                                 ctr[:, ch * CHUNK + half * 512:ch * CHUNK + (half + 1) * 512],
                                 True, True,
                                 reads=(f"ctr{ch}", "c_eye16"), writes=("qR",))
                            b.copy("vector", crow[:, half * 512:(half + 1) * 512],
                                   pR[0:1, :], reads=("qR",), writes=("crow",))
                            b.mm(pR[:, :], c_ones_r[:, :],
                                 crow[:, half * 512:(half + 1) * 512], True, True,
                                 reads=("crow", "c_ones_r"), writes=("qR",))
                            b.copy("vector", csb[:, half * 512:(half + 1) * 512],
                                   pR[:, :], reads=("qR",), writes=("csb",))
                    for s in range(NI):
                        ws = s % 2
                        w1v = w1b[:, ws * NC_D * 128:(ws + 1) * NC_D * 128]
                        w3v = w3b[:, ws * NC_D * 128:(ws + 1) * NC_D * 128]
                        b.dma("gpsimd", w1v, w1[e, s], reads=(f"w1b{ws}",), writes=(f"w1b{ws}",))
                        b.dma("gpsimd", w3v, w3[e, s], reads=(f"w3b{ws}",), writes=(f"w3b{ws}",))
                        for half in range(2):
                            pa, pb_ = pA[half], pB[half]
                            pan, pbn = f"qA{half}", f"qB{half}"
                            for c in range(NC_D):
                                rh = xn2c[:, c * CHUNK + half * 512:c * CHUNK + (half + 1) * 512]
                                b.mm(pa[:, :], w1v[:, c * 128:(c + 1) * 128], rh,
                                     c == 0, c == NC_D - 1,
                                     reads=(f"w1b{ws}", "xn2c"), writes=(pan,))
                            for c in range(NC_D):
                                rh = xn2c[:, c * CHUNK + half * 512:c * CHUNK + (half + 1) * 512]
                                b.mm(pb_[:, :], w3v[:, c * 128:(c + 1) * 128], rh,
                                     c == 0, c == NC_D - 1,
                                     reads=(f"w3b{ws}", "xn2c"), writes=(pbn,))
                            b.act(ssc[:, :], pa[:, :], AF.Silu, reads=(pan,), writes=("ssc",))
                            hdst = hsb[:, s * CHUNK + half * 512:s * CHUNK + (half + 1) * 512]
                            if routed:
                                b.tt("vector", h1c[:, :], ssc[:, :], pb_[:, :], ALU.mult,
                                     reads=("ssc", pbn), writes=("h1c",))
                                b.tt("vector", hdst, h1c[:, :],
                                     csb[:, half * 512:(half + 1) * 512], ALU.mult,
                                     reads=("h1c", "csb"), writes=(f"h_{s}",))
                            else:
                                b.tt("vector", hdst, ssc[:, :], pb_[:, :], ALU.mult,
                                     reads=("ssc", pbn), writes=(f"h_{s}",))
                    for s in range(NC_D):
                        ws = s % 3
                        w2v = w2b[:, ws * NI * 128:(ws + 1) * NI * 128]
                        b.dma("gpsimd", w2v, w2[e, s], reads=(f"w2b{ws}",), writes=(f"w2b{ws}",))
                        for half in range(2):
                            py = pY[half]
                            pyn = f"qY{half}"
                            for c in range(NI):
                                b.mm(py[:, :], w2v[:, c * 128:(c + 1) * 128],
                                     hsb[:, c * CHUNK + half * 512:c * CHUNK + (half + 1) * 512],
                                     c == 0, c == NI - 1,
                                     reads=(f"w2b{ws}", f"h_{c}"), writes=(pyn,))
                            ydst = yac[:, s * CHUNK + half * 512:s * CHUNK + (half + 1) * 512]
                            if e == 0:
                                b.copy("vector", ydst, py[:, :], reads=(pyn,), writes=(f"y_{s}",))
                            else:
                                b.tt("vector", ydst, ydst, py[:, :], ALU.add,
                                     reads=(pyn, f"y_{s}"), writes=(f"y_{s}",))
                if ch == 0:
                    ln4(0)
            ln4(NCH - 1)
            b.flush("phaseC")
        b.finalize()
    return nc



def _perm_rope(nheads):
    perm = np.zeros(nheads * 128, dtype=np.int64)
    for h in range(nheads):
        for j in range(128):
            src = 2 * j if j < 64 else 2 * (j - 64) + 1
            perm[h * 128 + j] = h * 128 + src
    return perm


def _tile_stream(wT, nslab, of_per_slab, nchunk):
    IN, OF = wT.shape
    assert IN == nchunk * 128 and OF == nslab * of_per_slab
    a = wT.reshape(nchunk, 128, nslab, of_per_slab)
    a = a.transpose(2, 1, 0, 3)
    return np.ascontiguousarray(a.reshape(nslab, 128, nchunk * of_per_slab))


def _prep(inputs):
    x = np.asarray(inputs["L0_attn_x"], np.float32)
    q_w = np.asarray(inputs["L0_attn_q_w"], np.float32)
    kv_w = np.asarray(inputs["L0_attn_kv_w"], np.float32)
    c_w = np.asarray(inputs["L0_attn_c_w"], np.float32)
    Wg = np.asarray(inputs["L0_ffn_Wg"], np.float32)
    W1 = np.asarray(inputs["L0_ffn_W1"], np.float32)
    W3 = np.asarray(inputs["L0_ffn_W3"], np.float32)
    W2 = np.asarray(inputs["L0_ffn_W2"], np.float32)
    Ws1 = np.asarray(inputs["L0_ffn_Ws1"], np.float32)
    Ws3 = np.asarray(inputs["L0_ffn_Ws3"], np.float32)
    Ws2 = np.asarray(inputs["L0_ffn_Ws2"], np.float32)

    permq = _perm_rope(NH)
    permk = _perm_rope(NKV)

    qwT = np.ascontiguousarray(q_w.T)[:, permq]
    qw_s = _tile_stream(qwT, NC_D, 128, NC_D)
    kvT = kv_w.T.copy()
    kvT[:, 0:512] = kvT[:, 0:512][:, permk]
    kv_s = _tile_stream(kvT, 8, 128, NC_D)
    cw_s = _tile_stream(np.ascontiguousarray(c_w.T), NC_D, 128, NC_D)
    wg_s = np.ascontiguousarray(Wg.T.reshape(NC_D, 128, NE))

    w1_all = np.concatenate([W1, Ws1.reshape(2, INTER, D)], 0)
    w3_all = np.concatenate([W3, Ws3.reshape(2, INTER, D)], 0)
    w2_all = np.concatenate(
        [W2, Ws2.T.reshape(2, INTER, D).transpose(0, 2, 1)], 0)

    w1_s = np.stack([_tile_stream(np.ascontiguousarray(w1_all[e].T), NI, 128, NC_D)
                     for e in range(NE + 2)])
    w3_s = np.stack([_tile_stream(np.ascontiguousarray(w3_all[e].T), NI, 128, NC_D)
                     for e in range(NE + 2)])
    w2_s = np.stack([_tile_stream(np.ascontiguousarray(w2_all[e].T), NC_D, 128, NI)
                     for e in range(NE + 2)])

    pairs = np.arange(64, dtype=np.float64)
    freqs = 1.0 / (10000.0 ** (2.0 * pairs / DKH))
    pos = np.arange(T, dtype=np.float64)
    ang = pos[None, :] * freqs[:, None]
    cos = np.cos(ang)
    sin = np.sin(ang)
    sc = 1.0 / np.sqrt(DKH)
    trig = np.stack([
        np.concatenate([cos, cos], 0) * sc,
        np.concatenate([-sin, sin], 0) * sc,
        np.concatenate([cos, cos], 0),
        np.concatenate([-sin, sin], 0),
    ]).astype(np.float32)

    ii = np.arange(128)
    causal = np.where(ii[None, :] > ii[:, None], -BIG, 0.0).astype(np.float32)

    shared = dict(qw=qw_s, kvw=kv_s, cw=cw_s, wg=wg_s, w1=w1_s, w3=w3_s, w2=w2_s,
                  trig=trig, causal=causal,
                  identf=np.eye(128, dtype=np.float32),
                  ones_r=np.ones((1, 128), np.float32),
                  ones_c=np.ones((128, 1), np.float32),
                  eye16=np.eye(16, dtype=np.float32),
                  constv=np.tile(np.array(CONSTVALS, np.float32), (128, 1)))
    in_maps = []
    for c in range(NB):
        m = dict(shared)
        m["xt"] = np.ascontiguousarray(
            x[c].T.reshape(NC_D, 128, T).transpose(1, 0, 2))
        in_maps.append(m)
    return in_maps


def kernel(**inputs):
    in_maps = _prep(inputs)
    nc = build_nc()
    res = run_bass_kernel_spmd(nc, in_maps, core_ids=list(range(NB)))
    outs = []
    for c in range(NB):
        o = np.asarray(res.results[c]["outy"], np.float32)
        outs.append(o.transpose(1, 0, 2).reshape(D, T).T)
    return np.stack(outs, 0)
